# version 07
# baseline (speedup 1.0000x reference)
import numpy as np
import contextlib
import concourse.bass as bass
import concourse.mybir as mybir
from concourse.bass_utils import run_bass_kernel_spmd

F32 = mybir.dt.float32
BF16 = mybir.dt.bfloat16
I32 = mybir.dt.int32
AF = mybir.ActivationFunctionType
ALU = mybir.AluOpType
AXX = mybir.AxisListType.X

D = 1024
L = 8192
TC = 1024
NCH = L // TC
DC = D // 128
SUB = 512
NSUB = TC // SUB
NCORE = 8
LN_EPS = 1e-5
LOG32 = float(np.log(32.0))

ENGINES = ("sync", "tensor", "vector", "scalar", "gpsimd")


class Sched:
    def __init__(self):
        self.q = {e: [] for e in ENGINES}
        self.cnt = {}

    def cur(self, sem):
        return self.cnt.get(id(sem), 0)

    def emit(self, eng, fn, waits=(), inc=None, amt=1):
        waits = tuple((s, v) for (s, v) in waits if s is not None and v)
        val = None
        if inc is not None:
            k = id(inc)
            self.cnt[k] = self.cnt.get(k, 0) + amt
            val = self.cnt[k]

        def run(h, fn=fn, waits=waits, inc=inc, amt=amt):
            for (s, v) in waits:
                h.wait_ge(s, v)
            ins = fn(h)
            if inc is not None:
                ins.then_inc(inc, amt)

        self.q[eng].append(run)
        return val


def mw(*ws):
    d = {}
    for w in ws:
        if w is None:
            continue
        s, v = w
        if s is None or not v:
            continue
        k = id(s)
        d[k] = (s, max(v, d[k][1])) if k in d else (s, v)
    return list(d.values())


class Prog:
    def __init__(self, S, env):
        self.S = S
        self.e = env
        self.guard = [None] * 8
        self.PS = env["PS"]

    def bank_waits(self, b, nb=1):
        return mw(*[self.guard[i] for i in range(b, b + nb) if self.guard[i]])

    def bank_done(self, b, nb, tok):
        for i in range(b, b + nb):
            self.guard[i] = tok

    def tt(self, eng, out, in0, in1, op, waits=(), inc=None):
        return self.S.emit(eng, lambda h: h.tensor_tensor(out=out, in0=in0, in1=in1, op=op),
                           waits=waits, inc=inc)

    def ts1(self, eng, out, in0, sc, op, waits=(), inc=None):
        return self.S.emit(eng, lambda h: h.tensor_single_scalar(out=out, in_=in0,
                                                                 scalar=sc, op=op),
                           waits=waits, inc=inc)

    def ts2(self, eng, out, in0, sa, sb, op0, op1, waits=(), inc=None):
        return self.S.emit(eng, lambda h: h.tensor_scalar(out=out, in0=in0, scalar1=sa,
                                                          scalar2=sb, op0=op0, op1=op1),
                           waits=waits, inc=inc)

    def stt(self, eng, out, in0, sc, in1, op0, op1, waits=(), inc=None):
        return self.S.emit(eng, lambda h: h.scalar_tensor_tensor(out=out, in0=in0, scalar=sc,
                                                                 in1=in1, op0=op0, op1=op1),
                           waits=waits, inc=inc)

    def act(self, out, in_, func, bias=0.0, scale=1.0, waits=(), inc=None):
        return self.S.emit("scalar", lambda h: h.activation(out=out, in_=in_, func=func,
                                                            bias=bias, scale=scale),
                           waits=waits, inc=inc)

    def mm(self, out, lhsT, rhs, start, stop, waits=(), inc=None):
        return self.S.emit("tensor", lambda h: h.matmul(out=out, lhsT=lhsT, rhs=rhs,
                                                        start=start, stop=stop),
                           waits=waits, inc=inc)

    def mmacc(self, out, pairs, waits=(), inc=None):
        n = len(pairs)
        tok = None
        for i, (lh, rh) in enumerate(pairs):
            tok = self.mm(out, lh, rh, i == 0, i == n - 1,
                          waits=(waits if i == 0 else ()),
                          inc=(inc if i == n - 1 else None))
        return tok

    def cp(self, eng, out, in_, waits=(), inc=None):
        return self.S.emit(eng, lambda h: h.tensor_copy(out=out, in_=in_),
                           waits=waits, inc=inc)

    def mset(self, eng, ap, v, waits=(), inc=None):
        return self.S.emit(eng, lambda h: h.memset(ap, v), waits=waits, inc=inc)

    def scan(self, out, d0, d1, initial, waits=(), inc=None):
        return self.S.emit("vector",
                           lambda h: h.tensor_tensor_scan(out=out, data0=d0, data1=d1,
                                                          initial=initial,
                                                          op0=ALU.mult, op1=ALU.add),
                           waits=waits, inc=inc)

    def rmax(self, out, in_, waits=(), inc=None):
        return self.S.emit("vector", lambda h: h.reduce_max(out=out, in_=in_, axis=AXX),
                           waits=waits, inc=inc)

    def rsum(self, out, in_, waits=(), inc=None):
        return self.S.emit("vector", lambda h: h.reduce_sum(out=out, in_=in_, axis=AXX),
                           waits=waits, inc=inc)

    def recip(self, out, in_, waits=(), inc=None):
        return self.S.emit("vector", lambda h: h.reciprocal(out=out, in_=in_),
                           waits=waits, inc=inc)

    def dma(self, eng, out, in_, waits=(), inc=None):
        if inc is not None:
            sem = inc
        elif eng == "gpsimd":
            sem = self.e["semG"]
        else:
            sem = self.e["semL"]
        v = self.S.emit(eng, lambda h: h.dma_start(out=out, in_=in_),
                        waits=waits, inc=sem, amt=16)
        return (sem, v)

    def transpose(self, out, in_, waits=(), inc=None):
        ident = self.e["ident_sb"]
        return self.S.emit("tensor", lambda h: h.transpose(out=out, in_=in_,
                                                           identity=ident[:]),
                           waits=waits, inc=inc)


def build_program(taps=()):
    nc = bass.Bass("TRN2", target_bir_lowering=False, debug=False)

    def din(name, shape, dt=F32):
        return nc.dram_tensor(name, list(shape), dt, kind="ExternalInput").ap()

    I = {}
    for nm, sh in [("x_fm", (D, L)),
                   ("WxT", (D, D)), ("WzT", (D, D)), ("WoT", (D, D)),
                   ("Wp_cat", (D, 320)), ("WdtT", (64, D)),
                   ("bdt_c", (128, DC)), ("convw", (128, DC, 4)),
                   ("A_alpha", (128, 128)), ("Dp_c", (128, DC)),
                   ("GU", (8, D, 2 * D)), ("GUbias", (128, 16, 8)),
                   ("DN", (8, D, D)), ("DNb", (8, D)),
                   ("rwT", (8, 2 * TC)),
                   ("Wqkva", (D, 3 * D)), ("gamma_c", (128, DC)),
                   ("WT1", (D, D)), ("WpT1", (D, 256)), ("bp_c", (128, 2)),
                   ("sim2", (128, 2, 4)), ("temp_in", (1, 1)),
                   ("WdcT", (D, 15)), ("WucT", (15, D)),
                   ("diag16", (128, 2048)), ("ydiag", (128, 2048)),
                   ("ones128", (128, 1)), ("ones1r", (1, 128)),
                   ("ecol", (8, 1)), ("pat_er", (4, 15)), ("ident", (128, 128))]:
        I[nm] = din(nm, sh)
    I["idxT"] = din("idxT", (2, 2 * TC), I32)

    y_out = nc.dram_tensor("y_fm", [D, L], F32, kind="ExternalOutput").ap()
    tap_aps = {}
    for tname, tshape, tdt in taps:
        tap_aps[tname] = nc.dram_tensor("tap_" + tname, list(tshape), tdt,
                                        kind="ExternalOutput").ap()

    HB = {}
    for nm, sh in [("WxTb", (D, D)), ("WzTb", (D, D)), ("WoTb", (D, D)),
                   ("Wqkvab", (D, 3 * D)), ("WT1b", (D, D)), ("WpT1b", (D, 256)),
                   ("GUb16", (8, D, 2 * D)), ("DNb16", (8, D, D))]:
        HB[nm] = nc.dram_tensor(nm, list(sh), BF16, kind="Internal").ap()

    S = Sched()

    allocs = [
        ("x_sb", [128, DC, TC], F32),
        ("xn_sb", [128, DC, TC], BF16),
        ("UN1", [128, 33824 + 8], BF16),
        ("SCRF", [128, 2048], F32),
        ("wstA", [128, DC, TC], BF16),
        ("wstB", [128, DC, TC], BF16),
        ("abuf", [128, 2, TC], F32),
        ("Pbuf", [128, TC], BF16),
        ("bbuf", [128, 2, TC], BF16),
        ("hbig", [128, 4, TC], BF16),
        ("BpRep", [128, TC], BF16),
        ("CRep", [128, TC], BF16),
        ("halo4", [128, DC, 4], BF16),
        ("hcarry", [128, 128], F32),
        ("kvcarry", [128, DC], F32),
        ("wp_sb", [128, DC, 320], BF16),
        ("wdt_sb", [64, D], BF16),
        ("diag_sb", [128, 2048], BF16),
        ("ydiag_sb", [128, 2048], BF16),
        ("vecs", [128, 32], F32),
        ("AA", [128, 128], F32),
        ("rAA", [128, 128], F32),
        ("onesb", [128, 1], BF16),
        ("ones1rb", [1, 128], BF16),
        ("ident_sb", [128, 128], BF16),
        ("gub_sb", [128, 16, 8], F32),
        ("dnb_sb", [8, D], BF16),
        ("cg_sb", [8, 2 * TC], BF16),
        ("wdc_sb", [128, DC, 15], BF16),
        ("wuc_sb", [15, D], BF16),
        ("bpv", [128, 2], F32),
        ("simn_sb", [128, 2, 4], BF16),
        ("lfr", [128, 128], F32),
        ("cbias", [128, 4], F32),
        ("patb", [4, 15], BF16),
    ]

    with contextlib.ExitStack() as st:
        env = {"nc": nc, "I": I, "HB": HB, "y_out": y_out, "tap_aps": tap_aps}
        for nm, sh, dt in allocs:
            env[nm] = st.enter_context(nc.sbuf_tensor(nm, sh, dt))
        env["PS"] = st.enter_context(nc.psum_tensor("PS", [128, 4096], F32))
        for s in ("semL", "semW", "semT", "semV", "semS", "semG", "semO"):
            env[s] = st.enter_context(nc.semaphore(s))

        P = Prog(S, env)
        _body(P)

        with nc.Block() as block:
            @block.sync
            def _(h):
                for f in S.q["sync"]:
                    f(h)

            @block.tensor
            def _(h):
                for f in S.q["tensor"]:
                    f(h)

            @block.vector
            def _(h):
                for f in S.q["vector"]:
                    f(h)

            @block.scalar
            def _(h):
                for f in S.q["scalar"]:
                    f(h)

            @block.gpsimd
            def _(h):
                for f in S.q["gpsimd"]:
                    f(h)

    return nc


def _body(P):
    S = P.S
    e = P.e
    I, HB, PS = e["I"], e["HB"], e["PS"]
    semL, semW, semT, semV, semS, semG, semO = (e["semL"], e["semW"], e["semT"],
                                                e["semV"], e["semS"], e["semG"], e["semO"])
    x_sb, xn_sb, UN1, SCRF = e["x_sb"], e["xn_sb"], e["UN1"], e["SCRF"]
    wstA, wstB = e["wstA"], e["wstB"]
    abuf, Pbuf, bbuf, hbig = e["abuf"], e["Pbuf"], e["bbuf"], e["hbig"]
    BpRep, CRep, halo4 = e["BpRep"], e["CRep"], e["halo4"]
    hcarry, kvcarry = e["hcarry"], e["kvcarry"]
    wp_sb, wdt_sb, diag_sb, ydiag_sb = e["wp_sb"], e["wdt_sb"], e["diag_sb"], e["ydiag_sb"]
    vecs, AA, rAA = e["vecs"], e["AA"], e["rAA"]
    onesb, ones1rb, ident_sb = e["onesb"], e["ones1rb"], e["ident_sb"]
    gub_sb, dnb_sb, cg_sb = e["gub_sb"], e["dnb_sb"], e["cg_sb"]
    wdc_sb, wuc_sb, bpv, simn_sb, lfr, patb = (e["wdc_sb"], e["wuc_sb"], e["bpv"],
                                               e["simn_sb"], e["lfr"], e["patb"])
    cbias = e["cbias"]
    y_out, tap_aps = e["y_out"], e["tap_aps"]

    STG = 0
    R0, R1, R2, R3 = 1024, 1024 + 8224, 1024 + 8224 + 8192, 1024 + 8224 + 2 * 8192

    def UN3(off):
        return UN1[:, off:off + DC * TC].rearrange("p (a b) -> p a b", a=DC)

    def UNdc(off, dc, n=TC):
        return UN1[:, off + dc * TC: off + dc * TC + n]

    def srow(c0, n):
        return SCRF[0:1, 1024 + c0:1024 + c0 + n]

    dtun = UN1[0:64, STG:STG + TC]
    CONVACC = SCRF[:, 0:TC]
    CONVW = lfr[:, 96:128].rearrange("p (a b) -> p a b", a=DC)

    wstAf = wstA[:].rearrange("p a b -> p (a b)").bitcast(F32)
    wstBv = wstB[:].rearrange("p a b -> p (a b)")

    vtok = 0
    P.mset("vector", cbias[:, 0:1], 1e-24)
    P.mset("vector", cbias[:, 1:2], LN_EPS)
    P.mset("vector", cbias[:, 2:3], LOG32)
    vtok = P.mset("vector", cbias[:, 3:4], 1.0, inc=semV)

    def small_load(dst_bf, src_ap, cols, rows=128):
        nonlocal vtok
        stg = wstAf[0:rows, 0:cols]
        t_ = P.dma("sync", stg, src_ap, waits=mw((semV, vtok)))
        vtok = P.cp("vector", dst_bf, stg, waits=[t_], inc=semV)
        return vtok

    stg3 = wstAf[:, 0:DC * 320].rearrange("p (a b) -> p a b", a=DC)
    t_ = P.dma("sync", stg3, I["Wp_cat"].rearrange("(a p) b -> p a b", p=128),
               waits=mw((semV, vtok)))
    vtok = P.cp("vector", wp_sb[:], stg3, waits=[t_], inc=semV)
    small_load(wdt_sb[:], I["WdtT"], 1024, 64)
    small_load(diag_sb[:], I["diag16"], 2048)
    small_load(ydiag_sb[:], I["ydiag"], 2048)
    small_load(onesb[:], I["ones128"], 1)
    small_load(ones1rb[:], I["ones1r"], 128, 1)
    small_load(ident_sb[:], I["ident"], 128)
    small_load(dnb_sb[:], I["DNb"], 1024, 8)
    small_load(patb[:], I["pat_er"], 15, 4)
    stg4 = wstAf[:, 0:DC * 15].rearrange("p (a b) -> p a b", a=DC)
    t_ = P.dma("sync", stg4, I["WdcT"].rearrange("(a p) b -> p a b", p=128),
               waits=mw((semV, vtok)))
    vtok = P.cp("vector", wdc_sb[:], stg4, waits=[t_], inc=semV)
    small_load(wuc_sb[:], I["WucT"], 1024, 15)
    P.dma("sync", vecs[:, 0:DC], I["bdt_c"])
    P.dma("sync", vecs[:, 8:8 + DC], I["Dp_c"])
    P.dma("sync", vecs[:, 16:16 + DC], I["gamma_c"])
    t_onf = P.dma("sync", vecs[:, 24:25], I["ones128"])
    t_ecl = P.dma("sync", vecs[0:8, 25:26], I["ecol"])
    t_cw2 = P.dma("sync", lfr[:, 96:128].rearrange("p (a b) -> p a b", a=DC), I["convw"])
    t_gub = P.dma("sync", gub_sb[:].rearrange("p a b -> p (a b)"),
                  I["GUbias"].rearrange("p a b -> p (a b)"))
    t_bp = P.dma("sync", bpv[:], I["bp_c"])

    t_aa = P.dma("sync", wstAf[0:128, 2560:2688], I["A_alpha"], waits=mw((semV, vtok)))
    s_aa = P.act(abuf[:, 0, 0:128], wstAf[0:128, 2560:2688], AF.Exp,
                 waits=[t_aa], inc=semS)
    v_aa1 = P.ts1("vector", AA[:], abuf[:, 0, 0:128], -1.0, ALU.mult,
                  waits=mw((semS, s_aa), (semV, vtok)), inc=semV)
    v_aa2 = P.ts2("vector", abuf[:, 0, 0:128], abuf[:, 0, 0:128], -1.0, 1e-8,
                  ALU.mult, ALU.add, waits=[(semS, s_aa)], inc=semV)
    v_aa3 = P.recip(rAA[:], abuf[:, 0, 0:128], waits=[(semV, v_aa2)], inc=semV)
    vtok = v_aa3

    vtok = P.mset("vector", hcarry[:], 0.0, waits=[(semV, vtok)], inc=semV)
    vtok = P.mset("vector", kvcarry[:], 0.0, waits=[(semV, vtok)], inc=semV)
    vtok = P.mset("vector", halo4[:].rearrange("p a b -> p (a b)"), 0.0,
                  waits=[(semV, vtok)], inc=semV)

    scrf_i = SCRF[0:1, 0:2 * TC].bitcast(I32)
    t_i0 = P.dma("sync", scrf_i, I["idxT"][0:1, :], waits=mw((semV, vtok)))
    v_i0 = P.cp("vector", UN1[0:1, 0:2 * TC], scrf_i, waits=[t_i0], inc=semV)
    t_i1 = P.dma("sync", scrf_i, I["idxT"][1:2, :], waits=[(semV, v_i0)])
    v_i1 = P.cp("vector", UN1[0:1, 2 * TC:4 * TC], scrf_i, waits=[t_i1], inc=semV)
    t_rw = P.dma("sync", SCRF[0:8, 0:2 * TC], I["rwT"], waits=[(semV, v_i1)])
    SCR8 = UN1[0:8, 8192:8192 + SUB]
    CNT8 = UN1[0:8, 10240:10240 + 2 * TC]
    last_cnt = None
    for k in range(2):
        srcrow = UN1[0:1, k * 2 * TC:(k + 1) * 2 * TC]
        for sub in range(4):
            w0 = mw((semV, v_i0), (semV, v_i1), t_ecl, *P.bank_waits(7, 1))
            mt = P.mm(PS[0:8, 7 * SUB:8 * SUB], ones1rb[0:1, 0:8],
                      srcrow[:, sub * SUB:(sub + 1) * SUB], True, True,
                      waits=w0, inc=semT)
            P.ts1("vector", SCR8, PS[0:8, 7 * SUB:8 * SUB],
                  vecs[0:8, 25:26], ALU.subtract, waits=[(semT, mt)])
            v_g = P.tt("vector", SCR8, SCR8, SCR8, ALU.mult, inc=semV)
            P.bank_done(7, 1, (semV, v_g))
            P.ts2("vector", SCR8, SCR8, -1.0, 1.0, ALU.mult, ALU.add)
            P.ts1("vector", SCR8, SCR8, 0.0, ALU.max)
            acc = CNT8[:, sub * SUB:(sub + 1) * SUB]
            if k == 0:
                last_cnt = P.cp("vector", acc, SCR8, inc=semV)
            else:
                last_cnt = P.tt("vector", acc, acc, SCR8, ALU.add, inc=semV)
    v_cg = P.tt("vector", cg_sb[:], CNT8, SCRF[0:8, 0:2 * TC],
                ALU.mult, waits=mw((semV, last_cnt), t_rw), inc=semV)

    t_sim = P.dma("sync", wstAf[:, 2688:2696].rearrange("p (a b) -> p a b", a=2),
                  I["sim2"])
    t_tmp = P.dma("sync", srow(512, 1), I["temp_in"])
    v_ssq = P.tt("vector", UN1[:, STG:STG + 8].rearrange("p (a b) -> p a b", a=2),
                 wstAf[:, 2688:2696].rearrange("p (a b) -> p a b", a=2),
                 wstAf[:, 2688:2696].rearrange("p (a b) -> p a b", a=2), ALU.mult,
                 waits=mw(t_sim, (semV, v_cg)), inc=semV)
    w_ = mw((semV, v_ssq), *P.bank_waits(6, 1))
    P.mm(PS[0:1, 6 * SUB:6 * SUB + 4], onesb[:], UN1[:, STG:STG + 4],
         True, False, waits=w_)
    m2_ = P.mm(PS[0:1, 6 * SUB:6 * SUB + 4], onesb[:], UN1[:, STG + 4:STG + 8],
               False, True, inc=semT)
    s_l1 = P.act(srow(516, 4), PS[0:1, 6 * SUB:6 * SUB + 4], AF.Ln, bias=cbias[0:1, 0:1],
                 waits=[(semT, m2_)], inc=semS)
    P.bank_done(6, 1, (semS, s_l1))
    s_e1 = P.act(srow(520, 4), srow(516, 4), AF.Exp, scale=-0.5,
                 waits=[(semS, s_l1)], inc=semS)
    v_tm = P.ts1("vector", srow(513, 1), srow(512, 1), float(np.log(100.0)), ALU.min,
                 waits=[t_tmp], inc=semV)
    s_tm = P.act(srow(514, 1), srow(513, 1), AF.Exp, waits=[(semV, v_tm)], inc=semS)
    v_rs = P.ts1("vector", srow(520, 4), srow(520, 4), srow(514, 1), ALU.mult,
                 waits=[(semS, s_e1), (semS, s_tm)], inc=semV)
    v_rsb = P.cp("vector", UN1[0:1, STG:STG + 4], srow(520, 4),
                 waits=[(semV, v_rs)], inc=semV)
    w_ = mw((semV, v_rsb), *P.bank_waits(6, 1))
    m3_ = P.mm(PS[:, 6 * SUB:6 * SUB + 4], ones1rb[:], UN1[0:1, STG:STG + 4],
               True, True, waits=w_, inc=semT)
    v_simn = P.tt("vector", simn_sb[:],
                  wstAf[:, 2688:2696].rearrange("p (a b) -> p a b", a=2),
                  PS[:, 6 * SUB:6 * SUB + 4][:, None, :].broadcast_to([128, 2, 4]),
                  ALU.mult, waits=[(semT, m3_)], inc=semV)
    P.bank_done(6, 1, (semV, v_simn))

    CV = 4096
    last_v = S.cur(semV)
    last_st = None
    for nm, nmb in [("WxT", "WxTb"), ("WzT", "WzTb"), ("WoT", "WoTb"),
                    ("Wqkva", "Wqkvab"), ("WT1", "WT1b"), ("WpT1", "WpT1b"),
                    ("GU", "GUb16"), ("DN", "DNb16")]:
        def _flat(ap):
            dims = "abcde"[:len(ap.shape)]
            return ap.rearrange(" ".join(dims) + " -> (" + " ".join(dims) + ")")
        flat_in = _flat(I[nm])
        flat_out = _flat(HB[nmb])
        sz = int(np.prod(I[nm].shape))
        assert sz % CV == 0
        nrow_tot = sz // CV
        o = 0
        while o < nrow_tot:
            rows = min(128, nrow_tot - o)
            t_in = P.dma("gpsimd", wstAf[0:rows, :],
                         flat_in[o * CV:(o + rows) * CV].rearrange("(p f) -> p f", f=CV),
                         waits=mw((semV, last_v)))
            last_v = P.cp("vector", wstBv[0:rows, 0:CV], wstAf[0:rows, :],
                          waits=mw(t_in, last_st), inc=semV)
            last_st = P.dma("gpsimd",
                            flat_out[o * CV:(o + rows) * CV].rearrange("(p f) -> p f", f=CV),
                            wstBv[0:rows, 0:CV], waits=[(semV, last_v)], inc=semW)
            o += rows
    WCONV = last_st
    wst_guard = {id(wstA): mw((semV, last_v)), id(wstB): mw(WCONV)}

    def tap_hbm(name, ap, *waits):
        if name in tap_aps:
            tk = P.dma("gpsimd", tap_aps[name], ap, waits=mw(*waits), inc=semO)
            S.emit("vector", lambda h: h.engine_nop(), waits=[tk], inc=semV)
            S.emit("tensor", lambda h: h.nop(), waits=[tk], inc=semT)
    tap_hbm("rbW", HB["Wqkvab"][:, 1024:2048].rearrange("(a p) b -> p a b", p=128),
            WCONV)

    def load_weight(buf, hb_ap, extra=()):
        cols = hb_ap.shape[1]
        gw = wst_guard.get(id(buf), ())
        return P.dma("sync", buf[:, :, 0:cols],
                     hb_ap.rearrange("(a p) b -> p a b", p=128),
                     waits=mw(*gw, *extra, WCONV))

    def set_wguard(buf, tok):
        wst_guard[id(buf)] = mw(tok)

    def emit_ln(src3, dst3, dep, src_f32, add_to=None, sq_off=R1):
        ones_f = vecs[:, 24:25]
        sq3 = UN3(sq_off)
        v_sq = P.tt("vector", sq3, src3, src3, ALU.mult, waits=dep, inc=semV)
        out_tok = None
        for sub in range(NSUB):
            sl = slice(sub * SUB, (sub + 1) * SUB)
            w6 = mw(*P.bank_waits(6, 2), (semV, v_sq), t_onf)
            for dc in range(DC):
                P.mm(PS[0:1, 6 * SUB:7 * SUB], (ones_f if src_f32 else onesb[:]),
                     src3[:, dc, sl], dc == 0, dc == DC - 1,
                     waits=(w6 if dc == 0 else ()))
            t_q = None
            for dc in range(DC):
                t_q = P.mm(PS[0:1, 7 * SUB:8 * SUB], onesb[:], sq3[:, dc, sl],
                           dc == 0, dc == DC - 1,
                           inc=(semT if dc == DC - 1 else None))
            mrow = srow(0, SUB)
            qrow = srow(512, SUB)
            v_m = P.ts1("vector", mrow, PS[0:1, 6 * SUB:7 * SUB], 1.0 / D, ALU.mult,
                        waits=[(semT, t_q)], inc=semV)
            v_q = P.ts1("vector", qrow, PS[0:1, 7 * SUB:8 * SUB], 1.0 / D, ALU.mult,
                        inc=semV)
            P.bank_done(6, 2, (semV, v_q))
            v_mm = P.tt("vector", SCRF[0:1, 0:512], mrow, mrow, ALU.mult,
                        waits=[(semV, v_q)], inc=semV)
            v_var = P.tt("vector", qrow, qrow, SCRF[0:1, 0:512], ALU.subtract,
                         waits=[(semV, v_mm)], inc=semV)
            s_ln = P.act(qrow, qrow, AF.Ln, bias=cbias[0:1, 1:2], waits=[(semV, v_var)], inc=semS)
            s_rstd = P.act(qrow, qrow, AF.Exp, scale=-0.5, waits=[(semS, s_ln)],
                           inc=semS)
            v_ng = P.stt("vector", mrow, mrow, -1.0, qrow, ALU.mult, ALU.mult,
                         waits=[(semS, s_rstd)], inc=semV)
            P.cp("vector", UN1[0:1, STG:STG + SUB], qrow)
            v_st2 = P.cp("vector", UN1[0:1, STG + SUB:STG + 2 * SUB], mrow,
                         waits=[(semV, v_ng)], inc=semV)
            w4 = mw(*P.bank_waits(4, 2), (semV, v_st2))
            P.mm(PS[:, 4 * SUB:5 * SUB], ones1rb[:], UN1[0:1, STG:STG + SUB],
                 True, True, waits=w4)
            mb2 = P.mm(PS[:, 5 * SUB:6 * SUB], ones1rb[:],
                       UN1[0:1, STG + SUB:STG + 2 * SUB], True, True, inc=semT)
            rbc = PS[:, 4 * SUB:5 * SUB][:, None, :].broadcast_to([128, DC, SUB])
            mbc = PS[:, 5 * SUB:6 * SUB][:, None, :].broadcast_to([128, DC, SUB])
            t1 = sq3[:, :, sl]
            P.tt("vector", t1, src3[:, :, sl], rbc, ALU.mult, waits=[(semT, mb2)])
            if add_to is None:
                out_tok = P.tt("vector", dst3[:, :, sl], t1, mbc, ALU.add, inc=semV)
            else:
                P.tt("vector", t1, t1, mbc, ALU.add)
                out_tok = P.tt("vector", add_to[:, :, sl], add_to[:, :, sl], t1,
                               ALU.add, inc=semV)
            P.bank_done(4, 2, (semV, out_tok))
        return (semV, out_tok)

    def tap(name, ap, *waits):
        if name in tap_aps:
            tk = P.dma("gpsimd", tap_aps[name], ap, waits=mw(*waits), inc=semO)
            S.emit("vector", lambda h: h.engine_nop(), waits=[tk], inc=semV)
            S.emit("tensor", lambda h: h.nop(), waits=[tk], inc=semT)

    prev_out = None
    abuf_guard = [None, None]
    gc_guard = [None, None]

    for ci in range(NCH):
        tsl = slice(ci * TC, (ci + 1) * TC)
        t_x = P.dma("sync", x_sb[:],
                    I["x_fm"][:, tsl].rearrange("(a p) b -> p a b", p=128),
                    waits=mw(prev_out))

        tok_xn = emit_ln(x_sb[:], xn_sb[:], [t_x], True, sq_off=R0)

        if ci == 0:
            tap("xn1", xn_sb[:], tok_xn)
        t_wx = load_weight(wstA, HB["WxTb"])
        t_wz = load_weight(wstB, HB["WzTb"])
        if ci == 0:
            tap("wA", wstA[:], t_wx)

        xpv3 = UN1[:, R0:R0 + DC * (TC + 4)].rearrange("p (a b) -> p a b", a=DC)
        v_halo = P.cp("vector", xpv3[:, :, 0:4], halo4[:],
                      waits=mw(tok_xn), inc=semV)
        xp_toks = []
        for dc in range(DC):
            for sub in range(NSUB):
                sl = slice(sub * SUB, (sub + 1) * SUB)
                b = (dc * NSUB + sub) % 2
                w0 = mw(tok_xn, t_wx, *P.bank_waits(b, 1))
                tk = P.mmacc(PS[:, b * SUB:(b + 1) * SUB],
                             [(wstA[:, k, dc * 128:(dc + 1) * 128], xn_sb[:, k, sl])
                              for k in range(DC)], waits=w0, inc=semT)
                vv = P.cp("vector", xpv3[:, dc, 4 + sub * SUB:4 + (sub + 1) * SUB],
                          PS[:, b * SUB:(b + 1) * SUB],
                          waits=mw((semT, tk), (semV, v_halo)), inc=semV)
                P.bank_done(b, 1, (semV, vv))
                xp_toks.append(vv)
        set_wguard(wstA, (semT, S.cur(semT)))
        sil_toks = []
        for dc in range(DC):
            for sub in range(NSUB):
                sl = slice(sub * SUB, (sub + 1) * SUB)
                b = 2 + (dc * NSUB + sub) % 2
                w0 = mw(tok_xn, t_wz, *P.bank_waits(b, 1))
                tk = P.mmacc(PS[:, b * SUB:(b + 1) * SUB],
                             [(wstB[:, k, dc * 128:(dc + 1) * 128], xn_sb[:, k, sl])
                              for k in range(DC)], waits=w0, inc=semT)
                ss = P.act(UNdc(R3, dc)[:, sl], PS[:, b * SUB:(b + 1) * SUB], AF.Silu,
                           waits=[(semT, tk)], inc=semS)
                P.bank_done(b, 1, (semS, ss))
                sil_toks.append(ss)
        set_wguard(wstB, (semT, S.cur(semT)))

        def wp_mm(col0, mout, dst_ap):
            last = None
            for sub in range(NSUB):
                sl = slice(sub * SUB, (sub + 1) * SUB)
                w0 = mw(tok_xn, *P.bank_waits(6, 1))
                tk = P.mmacc(PS[0:mout, 6 * SUB:7 * SUB],
                             [(wp_sb[:, k, col0:col0 + mout], xn_sb[:, k, sl])
                              for k in range(DC)], waits=w0, inc=semT)
                last = P.cp("vector", dst_ap[:, sl], PS[0:mout, 6 * SUB:7 * SUB],
                            waits=[(semT, tk)], inc=semV)
                P.bank_done(6, 1, (semV, last))
            return last

        v_dtun = wp_mm(0, 64, dtun[:])
        v_bp = wp_mm(64, 128, BpRep[:])
        v_cr = wp_mm(192, 128, CRep[:])

        dt_toks = []
        for dc in range(DC):
            for sub in range(NSUB):
                sl = slice(sub * SUB, (sub + 1) * SUB)
                b = 2 + (dc * NSUB + sub) % 2
                w0 = mw((semV, v_dtun), *P.bank_waits(b, 1))
                tk = P.mm(PS[:, b * SUB:(b + 1) * SUB],
                          wdt_sb[:, dc * 128:(dc + 1) * 128], dtun[:, sl],
                          True, True, waits=w0, inc=semT)
                raw = SCRF[:, 0:SUB]
                rawc = SCRF[:, SUB:2 * SUB]
                P.ts1("vector", raw, PS[:, b * SUB:(b + 1) * SUB],
                      vecs[:, dc:dc + 1], ALU.add, waits=[(semT, tk)])
                v_rc = P.ts1("vector", rawc, raw, 30.0, ALU.min, inc=semV)
                P.bank_done(b, 1, (semV, v_rc))
                s_e = P.act(rawc, rawc, AF.Exp, waits=[(semV, v_rc)], inc=semS)
                s_p = P.act(rawc, rawc, AF.Ln, bias=cbias[:, 3:4],
                            waits=[(semS, s_e)], inc=semS)
                ss = P.tt("vector", UNdc(R1, dc)[:, sl], rawc, raw, ALU.max,
                          waits=[(semS, s_p)], inc=semV)
                dt_toks.append(ss)

        u_toks = []
        for dc in range(DC):
            xpd = xpv3[:, dc, :]
            w0 = mw((semV, xp_toks[dc * NSUB + NSUB - 1]), t_cw2,
                    ((semS, u_toks[dc - 1]) if dc > 0 else None))
            P.ts1("vector", CONVACC, xpd[:, 1:1 + TC], CONVW[:, dc, 0:1], ALU.mult,
                  waits=w0)
            P.stt("vector", CONVACC, xpd[:, 2:2 + TC], CONVW[:, dc, 1:2], CONVACC,
                  ALU.mult, ALU.add)
            P.stt("vector", CONVACC, xpd[:, 3:3 + TC], CONVW[:, dc, 2:3], CONVACC,
                  ALU.mult, ALU.add)
            v_c = P.stt("vector", CONVACC, xpd[:, 4:4 + TC], CONVW[:, dc, 3:4], CONVACC,
                        ALU.mult, ALU.add, inc=semV)
            s_u = P.act(UNdc(R2, dc), CONVACC, AF.Silu, waits=[(semV, v_c)], inc=semS)
            u_toks.append(s_u)
        v_hsave = P.cp("vector", halo4[:], xpv3[:, :, TC:TC + 4], inc=semV)
        if ci == 0:
            tap("xproj", xpv3[:, :, 4:4 + TC], (semV, xp_toks[-1]))
            tap("dt", UN3(R1), (semV, dt_toks[-1]))
            tap("u", UN3(R2), (semS, u_toks[-1]))
            tap("sz", UN3(R3), (semS, sil_toks[-1]))
            tap("BpRep", BpRep[:], (semV, v_bp))
            tap("CRep", CRep[:], (semV, v_cr))
            tap("dtun", dtun, (semV, v_dtun))

        gate_toks = []
        for dc in range(DC):
            tkY = None
            for j in range(16):
                g = dc * 16 + j
                pb = j % 2
                dgl = diag_sb[:, j * 128:(j + 1) * 128]
                w0 = mw((semV, dt_toks[-1]), *P.bank_waits(0, 2))
                tk_dt = None
                for s_ in range(NSUB):
                    tk_dt = P.mm(PS[:, s_ * SUB:(s_ + 1) * SUB], dgl,
                                 UNdc(R1, dc)[:, s_ * SUB:(s_ + 1) * SUB], True, True,
                                 waits=(w0 if s_ == 0 else ()),
                                 inc=(semT if s_ == NSUB - 1 else None))
                w1 = mw((semS, u_toks[dc]), *P.bank_waits(2, 2))
                tk_u = None
                for s_ in range(NSUB):
                    tk_u = P.mm(PS[:, (2 + s_) * SUB:(3 + s_) * SUB], dgl,
                                UNdc(R2, dc)[:, s_ * SUB:(s_ + 1) * SUB], True, True,
                                waits=(w1 if s_ == 0 else ()),
                                inc=(semT if s_ == NSUB - 1 else None))
                s_a = P.act(abuf[:, pb, :], PS[:, 0:2 * SUB], AF.Exp,
                            scale=AA[:, g:g + 1],
                            waits=mw((semT, tk_dt), abuf_guard[pb]), inc=semS)
                P.bank_done(0, 2, (semS, s_a))
                v_P = P.stt("vector", Pbuf[:], BpRep[:], rAA[:, g:g + 1],
                            PS[:, 2 * SUB:4 * SUB], ALU.mult, ALU.mult,
                            waits=mw((semT, tk_u), (semV, v_bp)), inc=semV)
                P.bank_done(2, 2, (semV, v_P))
                v_b = P.stt("vector", bbuf[:, pb, :], abuf[:, pb, :], 1.0,
                            Pbuf[:], ALU.subtract, ALU.mult,
                            waits=mw((semS, s_a), gc_guard[pb]), inc=semV)
                v_h = P.scan(hbig[:, j % 4, :], abuf[:, pb, :], bbuf[:, pb, :],
                             hcarry[:, g:g + 1], inc=semV)
                if ci == 0 and g == 3:
                    tap("a3", abuf[:, pb, :], (semS, s_a))
                    tap("P3", Pbuf[:], (semV, v_P))
                    tap("b3", bbuf[:, pb, :], (semV, v_b))
                    tap("h3", hbig[:, j % 4, :], (semV, v_h))
                abuf_guard[pb] = (semV, v_h)
                v_gc = P.tt("vector", bbuf[:, pb, :], hbig[:, j % 4, :], CRep[:],
                            ALU.mult, waits=mw((semV, v_cr), (semV, v_h)), inc=semV)
                ydg = ydiag_sb[:, j * 128:(j + 1) * 128]
                wY = mw((semV, v_gc), *(P.bank_waits(4, 2) if j == 0 else ()))
                for s_ in range(NSUB):
                    tkY = P.mm(PS[:, (4 + s_) * SUB:(5 + s_) * SUB], ydg,
                               bbuf[:, pb, s_ * SUB:(s_ + 1) * SUB],
                               j == 0, j == 15,
                               waits=(wY if s_ == 0 else ()),
                               inc=(semT if s_ == NSUB - 1 else None))
                gc_guard[pb] = (semT, tkY)
                if j % 4 == 3:
                    P.cp("vector", hcarry[:, g - 3:g + 1], hbig[:, :, TC - 1],
                         waits=mw((semV, v_h)), inc=semV)
            v_g1 = P.stt("vector", abuf[:, 0, :], UNdc(R2, dc), vecs[:, 8 + dc:9 + dc],
                         PS[:, 4 * SUB:6 * SUB], ALU.mult, ALU.add,
                         waits=mw((semT, tkY), (semV, v_hsave), abuf_guard[0],
                                  (semS, sil_toks[-1])), inc=semV)
            P.bank_done(4, 2, (semV, v_g1))
            v_g2 = P.tt("vector", UNdc(R0, dc), abuf[:, 0, :], UNdc(R3, dc),
                        ALU.mult, inc=semV)
            abuf_guard[0] = (semV, v_g2)
            gate_toks.append(v_g2)
        yq3 = UN3(R0)
        if ci == 0:
            tap("yq", yq3, (semV, gate_toks[-1]))
            tap("hc", hcarry[:], (semV, S.cur(semV)))

        t_wo = load_weight(wstA, HB["WoTb"])
        bo1 = UN3(R1)
        bo_toks = []
        for dc in range(DC):
            for sub in range(NSUB):
                sl = slice(sub * SUB, (sub + 1) * SUB)
                b = (dc * NSUB + sub) % 2
                tk = P.mmacc(PS[:, b * SUB:(b + 1) * SUB],
                             [(wstA[:, k, dc * 128:(dc + 1) * 128], yq3[:, k, sl])
                              for k in range(DC)],
                             waits=mw((semV, gate_toks[-1]), t_wo,
                                      *P.bank_waits(b, 1)), inc=semT)
                vv = P.cp("vector", bo1[:, dc, sl], PS[:, b * SUB:(b + 1) * SUB],
                          waits=[(semT, tk)], inc=semV)
                P.bank_done(b, 1, (semV, vv))
                bo_toks.append(vv)
        set_wguard(wstA, (semT, S.cur(semT)))
        tok_b1 = emit_ln(bo1, None, mw((semV, bo_toks[-1])), False,
                         add_to=x_sb[:], sq_off=R2)

        if ci == 0:
            tap("x1", x_sb[:], tok_b1)
        if ci == 1:
            tap("c1_x1", x_sb[:], tok_b1)
            tap("c1_hc", hcarry[:], tok_b1)
        if ci < 2:
            tok_x2n = emit_ln(x_sb[:], xn_sb[:], mw(tok_b1), True, sq_off=R0)
            if ci == 0:
                tap("x2n", xn_sb[:], tok_x2n)
            DNst = abuf[:].rearrange("p a b -> p (a b)").bitcast(BF16)
            mo_toks = []
            prev_mcb = None
            for tq in range(2):
                tq_sl = slice(tq * SUB, (tq + 1) * SUB)
                gl_sl = slice(ci * TC + tq * SUB, ci * TC + (tq + 1) * SUB)
                inter_last = None
                for ex in range(8):
                    t_gg = load_weight(wstA, HB["GUb16"][ex, :, 0:D])
                    t_gu = load_weight(wstB, HB["GUb16"][ex, :, D:2 * D])
                    t_cr2 = P.dma("gpsimd", UN1[0:1, STG:STG + SUB], cg_sb[ex:ex + 1, gl_sl],
                                  waits=mw((semV, v_cg), tok_x2n, prev_mcb))
                    w6 = mw(t_cr2, *P.bank_waits(6, 1))
                    mcb = P.mm(PS[:, 6 * SUB:7 * SUB], ones1rb[:], UN1[0:1, STG:STG + SUB],
                               True, True, waits=w6, inc=semT)
                    prev_mcb = (semT, mcb)
                    v2 = None
                    for mp in range(DC):
                        w4 = mw(tok_x2n, t_gg, *P.bank_waits(4, 1))
                        tkg = P.mmacc(PS[:, 4 * SUB:5 * SUB],
                                      [(wstA[:, k, mp * 128:(mp + 1) * 128],
                                        xn_sb[:, k, tq_sl]) for k in range(DC)],
                                      waits=w4, inc=semT)
                        w5 = mw(t_gu, *P.bank_waits(5, 1))
                        tku = P.mmacc(PS[:, 5 * SUB:6 * SUB],
                                      [(wstB[:, k, mp * 128:(mp + 1) * 128],
                                        xn_sb[:, k, tq_sl]) for k in range(DC)],
                                      waits=w5, inc=semT)
                        gt = SCRF[:, 0:SUB]
                        v1 = P.ts2("vector", gt, PS[:, 4 * SUB:5 * SUB],
                                   gub_sb[:, mp, ex:ex + 1], 7.0, ALU.add, ALU.min,
                                   waits=[(semT, tkg)], inc=semV)
                        P.bank_done(4, 1, (semV, v1))
                        s_sg = P.act(SCRF[:, SUB:2 * SUB], gt, AF.Sigmoid, scale=1.702,
                                     waits=[(semV, v1)], inc=semS)
                        v2 = P.tt("vector", SCRF[:, SUB:2 * SUB], SCRF[:, SUB:2 * SUB],
                                  PS[:, 6 * SUB:7 * SUB], ALU.mult,
                                  waits=mw((semS, s_sg), (semT, mcb)), inc=semV)
                        P.tt("vector", SCRF[:, SUB:2 * SUB], gt, SCRF[:, SUB:2 * SUB],
                             ALU.mult)
                        v4 = P.ts2("vector", SCRF[:, 1024:1536], PS[:, 5 * SUB:6 * SUB],
                                   gub_sb[:, 8 + mp, ex:ex + 1], 7.0, ALU.add, ALU.min,
                                   waits=[(semT, tku)], inc=semV)
                        P.bank_done(5, 1, (semV, v4))
                        P.ts2("vector", SCRF[:, 1024:1536], SCRF[:, 1024:1536],
                              -7.0, 1.0, ALU.max, ALU.add)
                        slot = UN1[:, 1024 + (ex * 8 + mp) * SUB:1024 + (ex * 8 + mp + 1) * SUB]
                        inter_last = P.tt("vector", slot, SCRF[:, 1024:1536],
                                          SCRF[:, SUB:2 * SUB], ALU.mult, inc=semV)
                        if ci == 0 and tq == 0 and ex == 0 and mp == 0:
                            tap("int00", slot, (semV, inter_last))
                    P.bank_done(6, 1, (semV, v2))
                    set_wguard(wstA, (semT, S.cur(semT)))
                    set_wguard(wstB, (semT, S.cur(semT)))
                if ci == 0 and tq == 0:
                    tap("slots", UN1[:, 1024:1024 + 64 * SUB], (semV, inter_last))
                for half in range(2):
                    dn_done = (semT, S.cur(semT))
                    for ex in range(8):
                        t_dn = P.dma("sync",
                                     DNst[:, 0:4096].rearrange("p (a b) -> p a b", a=DC),
                                     HB["DNb16"][ex, :, half * 512:(half + 1) * 512]
                                     .rearrange("(a p) b -> p a b", p=128),
                                     waits=mw(dn_done, abuf_guard[0], abuf_guard[1]))
                        if ci == 0 and tq == 0 and half == 0 and ex == 0:
                            tap("dnw", DNst[:, 0:4096], t_dn)
                        abuf_guard[0] = abuf_guard[1] = None
                        tkdn = None
                        for mh in range(4):
                            first = (ex == 0)
                            w0 = mw((semV, inter_last), t_dn,
                                    *(P.bank_waits(mh, 1) if first else ()))
                            for mp in range(DC):
                                tkdn = P.mm(PS[:, mh * SUB:(mh + 1) * SUB],
                                            DNst[:, mp * 512 + mh * 128:mp * 512 + (mh + 1) * 128],
                                            UN1[:, 1024 + (ex * 8 + mp) * SUB:1024 + (ex * 8 + mp + 1) * SUB],
                                            first and mp == 0, mp == DC - 1,
                                            waits=(w0 if mp == 0 else ()),
                                            inc=(semT if (mh == 3 and mp == DC - 1) else None))
                        dn_done = (semT, tkdn)
                    for mh in range(4):
                        m = half * 4 + mh
                        tk = P.mm(PS[:, mh * SUB:(mh + 1) * SUB],
                                  dnb_sb[:, m * 128:(m + 1) * 128], cg_sb[:, gl_sl],
                                  False, True, inc=semT)
                        vv = P.cp("vector", xn_sb[:, m, tq_sl],
                                  PS[:, mh * SUB:(mh + 1) * SUB],
                                  waits=[(semT, tk)], inc=semV)
                        P.bank_done(mh, 1, (semV, vv))
                        mo_toks.append(vv)
                    abuf_guard[0] = (semT, S.cur(semT))
                    abuf_guard[1] = (semT, S.cur(semT))
            if ci == 0:
                tap("cg", cg_sb[:], (semV, v_cg))
                tap("moe", xn_sb[:], (semV, mo_toks[-1]))
            tok_blk2 = emit_ln(xn_sb[:], None, mw((semV, mo_toks[-1])), False,
                               add_to=x_sb[:], sq_off=R0)
        else:
            tok_blk2 = tok_b1

        if ci == 0:
            tap("x2", x_sb[:], tok_blk2)
        sq3 = UN3(R1)
        v_sq = P.tt("vector", sq3, x_sb[:], x_sb[:], ALU.mult, waits=mw(tok_blk2),
                    inc=semV)
        x3_toks = []
        for sub in range(NSUB):
            sl = slice(sub * SUB, (sub + 1) * SUB)
            w6 = mw((semV, v_sq), *P.bank_waits(6, 1))
            tq_ = None
            for dc in range(DC):
                tq_ = P.mm(PS[0:1, 6 * SUB:7 * SUB], onesb[:], sq3[:, dc, sl],
                           dc == 0, dc == DC - 1,
                           waits=(w6 if dc == 0 else ()),
                           inc=(semT if dc == DC - 1 else None))
            s_ln = P.act(srow(0, SUB), PS[0:1, 6 * SUB:7 * SUB], AF.Ln,
                         bias=cbias[0:1, 0:1], waits=[(semT, tq_)], inc=semS)
            P.bank_done(6, 1, (semS, s_ln))
            s_rn = P.act(srow(0, SUB), srow(0, SUB), AF.Exp, scale=-0.5,
                         bias=cbias[0:1, 2:3], waits=[(semS, s_ln)], inc=semS)
            v_st = P.cp("vector", UN1[0:1, STG:STG + SUB], srow(0, SUB),
                        waits=[(semS, s_rn)], inc=semV)
            w4 = mw((semV, v_st), *P.bank_waits(4, 1))
            mb = P.mm(PS[:, 4 * SUB:5 * SUB], ones1rb[:], UN1[0:1, STG:STG + SUB],
                      True, True, waits=w4, inc=semT)
            vv = None
            for dc in range(DC):
                vv = P.stt("vector", xn_sb[:, dc, sl], x_sb[:, dc, sl],
                           vecs[:, 16 + dc:17 + dc], PS[:, 4 * SUB:5 * SUB],
                           ALU.mult, ALU.mult,
                           waits=([(semT, mb)] if dc == 0 else ()), inc=semV)
            P.bank_done(4, 1, (semV, vv))
            x3_toks.append(vv)
        if ci == 0:
            tap("xn3", xn_sb[:], (semV, x3_toks[-1]))
        y3 = UN3(R2)
        kv_all = UN3(R3)
        if ci == 1:
            tap("c2carry", kvcarry[:], (semV, S.cur(semV)))
        t_ag = load_weight(wstA, HB["Wqkvab"][:, 2 * D:3 * D])
        t_kv = load_weight(wstB, HB["Wqkvab"][:, D:2 * D])
        if ci == 0:
            tap("wstAag", wstA[:], t_ag)
            tap("wstBkv", wstB[:], t_kv)
        kv_toks = []
        for dc in range(DC):
            s_a = None
            for sub in range(NSUB):
                sl = slice(sub * SUB, (sub + 1) * SUB)
                b = sub
                w0 = mw((semV, x3_toks[-1]), t_ag, *P.bank_waits(b, 1),
                        abuf_guard[1])
                tk = P.mmacc(PS[:, b * SUB:(b + 1) * SUB],
                             [(wstA[:, k, dc * 128:(dc + 1) * 128], xn_sb[:, k, sl])
                              for k in range(DC)], waits=w0, inc=semT)
                s_a = P.act(abuf[:, 1, sl], PS[:, b * SUB:(b + 1) * SUB], AF.Sigmoid,
                            waits=[(semT, tk)], inc=semS)
                P.bank_done(b, 1, (semS, s_a))
            v_kvi = None
            for sub in range(NSUB):
                sl = slice(sub * SUB, (sub + 1) * SUB)
                b = 2 + sub
                w0 = mw(t_kv, *P.bank_waits(b, 1), abuf_guard[0])
                tk = P.mmacc(PS[:, b * SUB:(b + 1) * SUB],
                             [(wstB[:, k, dc * 128:(dc + 1) * 128], xn_sb[:, k, sl])
                              for k in range(DC)], waits=w0, inc=semT)
                v_kvi = P.cp("vector", bbuf[:, 0, sl], PS[:, b * SUB:(b + 1) * SUB],
                             waits=[(semT, tk)], inc=semV)
                P.bank_done(b, 1, (semV, v_kvi))
            v_kv = P.scan(kv_all[:, dc, :], abuf[:, 1, :], bbuf[:, 0, :],
                          kvcarry[:, dc:dc + 1],
                          waits=mw((semS, s_a), (semV, v_kvi)), inc=semV)
            abuf_guard[0] = abuf_guard[1] = (semV, v_kv)
            if ci == 0 and dc == 0:
                tap("a0", abuf[:, 1, :], (semS, s_a))
                tap("kvi0", bbuf[:, 0, :], (semV, v_kvi))
                tap("kv0", kv_all[:, dc, :], (semV, v_kv))
            if ci == 1 and dc == 7:
                tap("c1a7", abuf[:, 1, :], (semS, s_a))
                tap("c1kvi7", bbuf[:, 0, :], (semV, v_kvi))
                tap("c1kv7", kv_all[:, dc, :], (semV, v_kv))
            kv_toks.append(v_kv)
        P.cp("vector", kvcarry[:], kv_all[:, :, TC - 1],
             waits=mw((semV, kv_toks[-1])), inc=semV)
        set_wguard(wstA, (semT, S.cur(semT)))
        set_wguard(wstB, (semT, S.cur(semT)))
        if ci == 0:
            tap("c1carry", kvcarry[:], (semV, S.cur(semV)))
        if ci == 0:
            tap("kv", kv_all, (semV, kv_toks[-1]))
        t_q = load_weight(wstA, HB["Wqkvab"][:, 0:D])
        y3_toks = []
        for dc in range(DC):
            for sub in range(NSUB):
                sl = slice(sub * SUB, (sub + 1) * SUB)
                b = (dc * NSUB + sub) % 2
                w0 = mw(t_q, (semV, kv_toks[dc]), *P.bank_waits(b, 1))
                tk = P.mmacc(PS[:, b * SUB:(b + 1) * SUB],
                             [(wstA[:, k, dc * 128:(dc + 1) * 128], xn_sb[:, k, sl])
                              for k in range(DC)], waits=w0, inc=semT)
                vv = P.tt("vector", y3[:, dc, sl], PS[:, b * SUB:(b + 1) * SUB],
                          kv_all[:, dc, sl], ALU.mult, waits=[(semT, tk)], inc=semV)
                P.bank_done(b, 1, (semV, vv))
                y3_toks.append(vv)
        set_wguard(wstA, (semT, S.cur(semT)))
        if ci == 0:
            tap("y3", y3, (semV, y3_toks[-1]))
        tok_b3 = emit_ln(y3, None, mw((semV, y3_toks[-1])), False,
                         add_to=x_sb[:], sq_off=R1)

        if ci == 0:
            tap("x3", x_sb[:], tok_b3)
        if ci == 1:
            tap("c1_x3", x_sb[:], tok_b3)
        tok_x4n = emit_ln(x_sb[:], xn_sb[:], mw(tok_b3), True, sq_off=R1)
        if ci == 0:
            tap("xn4", xn_sb[:], tok_x4n)
        projv = UN1[:, R2:R2 + 2 * TC].rearrange("p (a b) -> p a b", a=2)
        t_wp1 = load_weight(wstB, HB["WpT1b"])
        pj_toks = []
        for mo in range(2):
            for sub in range(NSUB):
                sl = slice(sub * SUB, (sub + 1) * SUB)
                b = 2 + (mo * NSUB + sub) % 2
                w0 = mw(tok_x4n, t_wp1, *P.bank_waits(b, 1))
                tk = P.mmacc(PS[:, b * SUB:(b + 1) * SUB],
                             [(wstB[:, k, mo * 128:(mo + 1) * 128], xn_sb[:, k, sl])
                              for k in range(DC)], waits=w0, inc=semT)
                ss = P.act(projv[:, mo, sl], PS[:, b * SUB:(b + 1) * SUB], AF.Identity,
                           bias=bpv[:, mo:mo + 1], waits=[(semT, tk)], inc=semS)
                P.bank_done(b, 1, (semS, ss))
                pj_toks.append(ss)
        set_wguard(wstB, (semT, S.cur(semT)))
        psq = UN1[:, R2 + 2 * TC:R2 + 4 * TC].rearrange("p (a b) -> p a b", a=2)
        v_psq = P.tt("vector", psq, projv, projv, ALU.mult,
                     waits=mw((semS, pj_toks[-1])), inc=semV)
        pn_toks = []
        for sub in range(NSUB):
            sl = slice(sub * SUB, (sub + 1) * SUB)
            w6 = mw((semV, v_psq), *P.bank_waits(6, 1))
            tq_ = None
            for ko in range(2):
                tq_ = P.mm(PS[0:1, 6 * SUB:7 * SUB], onesb[:], psq[:, ko, sl],
                           ko == 0, ko == 1, waits=(w6 if ko == 0 else ()),
                           inc=(semT if ko == 1 else None))
            s_ln = P.act(srow(0, SUB), PS[0:1, 6 * SUB:7 * SUB], AF.Ln,
                         bias=cbias[0:1, 0:1], waits=[(semT, tq_)], inc=semS)
            P.bank_done(6, 1, (semS, s_ln))
            s_rn = P.act(srow(0, SUB), srow(0, SUB), AF.Exp, scale=-0.5,
                         waits=[(semS, s_ln)], inc=semS)
            v_st = P.cp("vector", UN1[0:1, STG:STG + SUB], srow(0, SUB),
                        waits=[(semS, s_rn)], inc=semV)
            w4 = mw((semV, v_st), *P.bank_waits(4, 1))
            mb = P.mm(PS[:, 4 * SUB:5 * SUB], ones1rb[:], UN1[0:1, STG:STG + SUB],
                      True, True, waits=w4, inc=semT)
            rnb = PS[:, 4 * SUB:5 * SUB][:, None, :].broadcast_to([128, 2, SUB])
            vv = P.tt("vector", projv[:, :, sl], projv[:, :, sl], rnb, ALU.mult,
                      waits=[(semT, mb)], inc=semV)
            P.bank_done(4, 1, (semV, vv))
            pn_toks.append(vv)
        lg_toks = []
        for tile in range(8):
            w2 = mw((semV, pn_toks[tile // 4]), (semV, v_simn),
                    *(P.bank_waits(2, 1) if tile == 0 else ()))
            tk = P.mmacc(PS[:, 2 * SUB + tile * 4:2 * SUB + (tile + 1) * 4],
                         [(projv[:, k, tile * 128:(tile + 1) * 128], simn_sb[:, k, :])
                          for k in range(2)], waits=w2, inc=semT)
            lg_toks.append(tk)
        lg = PS[:, 2 * SUB:2 * SUB + 32].rearrange("p (a b) -> p a b", a=8)
        v_mx = P.rmax(lfr[:, 0:8], lg, waits=[(semT, lg_toks[-1])], inc=semV)
        v_sb = P.tt("vector", lfr[:, 8:40].rearrange("p (a b) -> p a b", a=8), lg,
                    lfr[:, 0:8].rearrange("p (a b) -> p a b", b=1)
                    .broadcast_to([128, 8, 4]), ALU.subtract,
                    waits=[(semV, v_mx)], inc=semV)
        P.bank_done(2, 1, (semV, v_sb))
        s_ex = P.act(lfr[:, 8:40], lfr[:, 8:40], AF.Exp, waits=[(semV, v_sb)], inc=semS)
        v_rs = P.rsum(lfr[:, 0:8], lfr[:, 8:40].rearrange("p (a b) -> p a b", a=8),
                      waits=[(semS, s_ex)], inc=semV)
        v_rc = P.recip(lfr[:, 0:8], lfr[:, 0:8], waits=[(semV, v_rs)], inc=semV)
        v_n = P.tt("vector", lfr[:, 8:40].rearrange("p (a b) -> p a b", a=8),
                   lfr[:, 8:40].rearrange("p (a b) -> p a b", a=8),
                   lfr[:, 0:8].rearrange("p (a b) -> p a b", b=1).broadcast_to([128, 8, 4]),
                   ALU.mult, waits=[(semV, v_rc)], inc=semV)
        v_m2 = P.rmax(lfr[:, 0:8], lfr[:, 8:40].rearrange("p (a b) -> p a b", a=8),
                      waits=[(semV, v_n)], inc=semV)
        v_eq = P.tt("vector", lfr[:, 40:72].rearrange("p (a b) -> p a b", a=8),
                    lfr[:, 8:40].rearrange("p (a b) -> p a b", a=8),
                    lfr[:, 0:8].rearrange("p (a b) -> p a b", b=1).broadcast_to([128, 8, 4]),
                    ALU.is_equal, waits=[(semV, v_m2)], inc=semV)
        v_ret = P.tt("vector", lfr[:, 8:40], lfr[:, 8:40], lfr[:, 40:72], ALU.mult,
                     waits=[(semV, v_eq)], inc=semV)
        retT = UN1[0:4, R0:R0 + TC]
        rt_last = None
        for tile in range(8):
            v_stg = P.cp("vector", UN1[:, STG + 1020:STG + 1024],
                         lfr[:, 8 + tile * 4:12 + tile * 4],
                         waits=mw((semV, v_ret) if tile == 0 else None), inc=semV)
            w3 = P.bank_waits(3, 1) if tile == 0 else ()
            ps_bf = PS[0:4, 3 * SUB + tile * 64:3 * SUB + (tile + 1) * 64].bitcast(BF16)
            tk = P.transpose(ps_bf, UN1[:, STG + 1020:STG + 1024],
                             waits=mw((semV, v_stg), *w3), inc=semT)
            rt_last = P.cp("vector", retT[:, tile * 128:(tile + 1) * 128], ps_bf,
                           waits=[(semT, tk)], inc=semV)
        P.bank_done(3, 1, (semV, rt_last))
        w4 = mw((semV, rt_last), *P.bank_waits(4, 2))
        tk_re = None
        for sub in range(NSUB):
            tk_re = P.mm(PS[0:15, (4 + sub) * SUB:(5 + sub) * SUB], patb[:],
                         retT[:, sub * SUB:(sub + 1) * SUB], True, True,
                         waits=(w4 if sub == 0 else ()),
                         inc=(semT if sub == 1 else None))
        dn15 = UN1[0:15, R1:R1 + TC]
        gat15 = UN1[0:15, R1 + TC:R1 + 2 * TC]
        w6 = mw(tok_x4n, *P.bank_waits(6, 2))
        tk_dn = None
        for sub in range(NSUB):
            sl = slice(sub * SUB, (sub + 1) * SUB)
            for k in range(DC):
                tk_dn = P.mm(PS[0:15, (6 + sub) * SUB:(7 + sub) * SUB],
                             wdc_sb[:, k, :], xn_sb[:, k, sl], k == 0, k == DC - 1,
                             waits=(w6 if (sub == 0 and k == 0) else ()),
                             inc=(semT if k == DC - 1 else None))
        P.act(dn15[:, 0:SUB], PS[0:15, 6 * SUB:7 * SUB], AF.Copy,
              waits=[(semT, tk_dn)])
        s_dn = P.act(dn15[:, SUB:TC], PS[0:15, 7 * SUB:8 * SUB], AF.Copy,
                     inc=semS)
        P.bank_done(6, 2, (semS, s_dn))
        P.tt("vector", gat15[:, 0:SUB], dn15[:, 0:SUB], PS[0:15, 4 * SUB:5 * SUB],
             ALU.mult, waits=mw((semS, s_dn), (semT, tk_re)))
        v_gt = P.tt("vector", gat15[:, SUB:TC], dn15[:, SUB:TC],
                    PS[0:15, 5 * SUB:6 * SUB], ALU.mult, inc=semV)
        P.bank_done(4, 2, (semV, v_gt))
        if ci == 0:
            tap("projn", projv, (semV, pn_toks[-1]))
            tap("retT", retT, (semV, rt_last))
            tap("dn15", dn15, (semS, s_dn))
            tap("gat15", gat15, (semV, v_gt))
        t_w1 = load_weight(wstA, HB["WT1b"])
        bo4 = UN3(R3)
        b4_toks = []
        for dc in range(DC):
            for sub in range(NSUB):
                sl = slice(sub * SUB, (sub + 1) * SUB)
                b = (dc * NSUB + sub) % 2
                w0 = mw(tok_x4n, t_w1, (semV, v_gt), *P.bank_waits(b, 1))
                for k in range(DC):
                    P.mm(PS[:, b * SUB:(b + 1) * SUB],
                         wstA[:, k, dc * 128:(dc + 1) * 128], xn_sb[:, k, sl],
                         k == 0, False, waits=(w0 if k == 0 else ()))
                tk = P.mm(PS[:, b * SUB:(b + 1) * SUB],
                          wuc_sb[0:15, dc * 128:(dc + 1) * 128], gat15[:, sl],
                          False, True, inc=semT)
                vv = P.cp("vector", bo4[:, dc, sl], PS[:, b * SUB:(b + 1) * SUB],
                          waits=[(semT, tk)], inc=semV)
                P.bank_done(b, 1, (semV, vv))
                b4_toks.append(vv)
        set_wguard(wstA, (semT, S.cur(semT)))
        if ci == 0:
            tap("bo4", bo4, (semV, b4_toks[-1]))
        tok_b4 = emit_ln(bo4, None, mw((semV, b4_toks[-1])), False,
                         add_to=x_sb[:], sq_off=R1)
        if ci == 1:
            tap("c1_x4", x_sb[:], tok_b4)

        t_out = P.dma("gpsimd", y_out[:, tsl].rearrange("(a p) b -> p a b", p=128),
                      x_sb[:], waits=mw(tok_b4), inc=semO)
        prev_out = t_out

    S.emit("sync", lambda h: h.nop(), waits=[(semO, S.cur(semO))])


def _prep_inputs(inputs, core):
    f32 = np.float32
    x = inputs["L0_attn_x"]

    def T(a):
        return np.ascontiguousarray(np.asarray(a).T).astype(f32)

    A_log = np.asarray(inputs["L0_attn_A_log"]).astype(f32)
    A_alpha = np.ascontiguousarray(
        A_log.reshape(128, 8, 16).transpose(1, 2, 0).reshape(128, 128)).astype(f32)
    Wp = np.asarray(inputs["L0_attn_Wp"]).astype(f32)
    WpT = Wp.T
    rep = np.arange(128) % 16
    Wp_cat = np.ascontiguousarray(
        np.concatenate([WpT[:, 0:64], WpT[:, 64 + rep], WpT[:, 80 + rep]], axis=1)
    ).astype(f32)
    diag16 = np.zeros((128, 2048), f32)
    ydiag = np.zeros((128, 2048), f32)
    for j in range(16):
        for dl in range(8):
            diag16[8 * j + dl, j * 128 + dl * 16:j * 128 + dl * 16 + 16] = 1.0
            ydiag[dl * 16 + np.arange(16), j * 128 + 8 * j + dl] = 1.0
    gup = np.asarray(inputs["L0_ffn_gate_up_proj"]).astype(f32)
    GU = np.ascontiguousarray(np.concatenate([gup[:, :, 0::2], gup[:, :, 1::2]], axis=2))
    gub = np.asarray(inputs["L0_ffn_gate_up_proj_bias"]).astype(f32)
    gub_de = np.concatenate([gub[:, 0::2], gub[:, 1::2]], axis=1)
    GUbias = np.ascontiguousarray(
        gub_de.reshape(8, 16, 128).transpose(2, 1, 0)).astype(f32)
    ri = np.asarray(inputs["L0_ffn_router_indices"]).astype(np.int32)
    rw = np.asarray(inputs["L0_ffn_routing_weights"]).astype(f32)
    if core == 0:
        idxT = np.ascontiguousarray(ri.T)
        rwT = np.ascontiguousarray(rw.T)
    else:
        idxT = np.zeros((2, 2048), np.int32)
        rwT = np.zeros((8, 2048), f32)
    Wd = np.concatenate([np.asarray(inputs["L1_ffn_Wd0"]), np.asarray(inputs["L1_ffn_Wd1"]),
                         np.asarray(inputs["L1_ffn_Wd2"]), np.asarray(inputs["L1_ffn_Wd3"])],
                        axis=0).astype(f32)
    Wu = np.concatenate([np.asarray(inputs["L1_ffn_Wu0"]), np.asarray(inputs["L1_ffn_Wu1"]),
                         np.asarray(inputs["L1_ffn_Wu2"]), np.asarray(inputs["L1_ffn_Wu3"])],
                        axis=1).astype(f32)
    pat = np.zeros((4, 15), f32)
    ro = 0
    for ei, r in enumerate([1, 2, 4, 8]):
        pat[ei, ro:ro + r] = 1.0
        ro += r
    sim = np.asarray(inputs["L1_ffn_sim"]).astype(f32)
    sim2 = np.ascontiguousarray(sim.reshape(2, 128, 4).transpose(1, 0, 2))

    return {
        "x_fm": T(x[core]),
        "WxT": T(inputs["L0_attn_Wx"]),
        "WzT": T(inputs["L0_attn_Wz"]),
        "WoT": T(inputs["L0_attn_Wo"]),
        "Wp_cat": Wp_cat,
        "WdtT": T(inputs["L0_attn_Wdt"]),
        "bdt_c": np.ascontiguousarray(
            np.asarray(inputs["L0_attn_bdt"]).astype(f32).reshape(DC, 128).T),
        "convw": np.ascontiguousarray(
            np.asarray(inputs["L0_attn_conv_w"]).astype(f32).reshape(D, 4)
            .reshape(DC, 128, 4).transpose(1, 0, 2)),
        "A_alpha": A_alpha,
        "Dp_c": np.ascontiguousarray(
            np.asarray(inputs["L0_attn_Dp"]).astype(f32).reshape(DC, 128).T),
        "GU": GU,
        "GUbias": GUbias,
        "DN": np.asarray(inputs["L0_ffn_down_proj"]).astype(f32),
        "DNb": np.asarray(inputs["L0_ffn_down_proj_bias"]).astype(f32),
        "idxT": idxT,
        "rwT": rwT,
        "Wqkva": np.asarray(inputs["L1_attn_w_qkva"]).astype(f32),
        "gamma_c": np.ascontiguousarray(
            np.asarray(inputs["L1_attn_gamma"]).astype(f32).reshape(DC, 128).T),
        "WT1": T(inputs["L1_ffn_W"]),
        "WpT1": T(inputs["L1_ffn_Wp"]),
        "bp_c": np.ascontiguousarray(
            np.asarray(inputs["L1_ffn_bp"]).astype(f32).reshape(2, 128).T),
        "sim2": sim2,
        "temp_in": np.asarray(inputs["L1_ffn_temp"]).astype(f32).reshape(1, 1),
        "WdcT": np.ascontiguousarray(Wd.T).astype(f32),
        "WucT": np.ascontiguousarray(Wu.T).astype(f32),
        "diag16": diag16,
        "ydiag": ydiag,
        "ones128": np.ones((128, 1), f32),
        "ones1r": np.ones((1, 128), f32),
        "ecol": np.arange(8, dtype=f32).reshape(8, 1),
        "pat_er": pat,
        "ident": np.eye(128, dtype=f32),
    }


def kernel(**inputs):
    nc = build_program()
    in_maps = [_prep_inputs(inputs, c) for c in range(NCORE)]
    res = run_bass_kernel_spmd(nc, in_maps, core_ids=list(range(NCORE)))
    out = np.empty((NCORE, L, D), np.float32)
    for c in range(NCORE):
        out[c] = res.results[c]["y_fm"].T
    return out
